# version 02
# baseline (speedup 1.0000x reference)
import numpy as np
import ml_dtypes
from contextlib import ExitStack

import concourse.bass as bass
import concourse.mybir as mybir
from concourse.bass_utils import run_bass_kernel_spmd

F32 = mybir.dt.float32
BF16 = mybir.dt.bfloat16
AF = mybir.ActivationFunctionType
ALU = mybir.AluOpType
AX = mybir.AxisListType

B, L, D = 4, 8192, 1024
T = 4096
NH, DQK, DV, VDIM = 8, 128, 256, 2048
NE, DFF = 8, 4096
H1, DH1 = 16, 64
CH = 128
NCH = T // CH
NT128 = T // 128
EPS = 1e-5
NCORE = 8

SCRATCH = {
    "xn0":   ([T, D], BF16),
    "w8T":   ([8, T], F32),
    "xg":    ([2 * T, D], BF16),
    "yslots": ([2 * T, D], F32),
    "wcol":  ([128, T // 128], F32),
    "sidx":  ([128, T // 128], mybir.dt.int32),
    "xn0T":  ([D, T], BF16),
    "xnp":   ([T, D], BF16),
    "xnpT":  ([D, T], BF16),
    "S_pre": ([128, NH * DV], F32),
    "attn0": ([T, D], BF16),
    "ffacc": ([T, D], F32),
    "x1":    ([T, D], BF16),
    "xn1":   ([T, D], BF16),
    "xn1T":  ([D, T], BF16),
    "qT1":   ([H1 * DH1, T], BF16),
    "NDa":   ([128, 512], F32),
    "NDb":   ([1, 1024], F32),
    "attn1": ([T, D], BF16),
}
OUTPUT = ("y", [T, D], F32)

PHASES = {
    "LN0":   (["x_main", "gate_w", "ident_f32", "Ltri", "eoff"], [],
              ["xn0", "xg", "wcol", "sidx"]),
    "T0":    ([], ["xn0"], ["xn0T"]),
    "PRELN": (["x_pre"], [], ["xnp"]),
    "PRET":  ([], ["xnp"], ["xnpT"]),
    "PRE0":  (["wk", "wv", "cosp", "sinp", "wdec"], ["xnpT"], ["S_pre"]),
    "ATT0":  (["wq", "wk", "wv", "wg", "wo", "cosm", "sinm", "cosTm", "sinTm",
               "dmask", "gpow", "wdec", "gnwb", "smask", "ident_f32"],
              ["xn0T", "S_pre"], ["attn0"]),
    "MOE0S": (["w1", "b1p", "w2", "b2row"], ["xg", "sidx", "wcol"], ["ffacc", "yslots"]),
    "COMB0": (["x_main"], ["attn0", "ffacc"], ["x1", "xn1"]),
    "T1":    ([], ["xn1"], ["xn1T"]),
    "ATT1A": (["wqkv", "ident_f32"], ["xn1T"], ["qT1", "NDa", "NDb"]),
    "ATT1B": (["wout"], ["qT1", "xn1T", "NDa", "NDb", "NDa_sib", "NDb_sib"], ["attn1"]),
    "FFN1":  (["w1T", "b1fp", "w2T", "b2f"], ["xn1T", "attn1", "x1"], ["y"]),
}

ALL_PHASES = ["LN0", "T0", "PRELN", "PRET", "PRE0", "ATT0", "MOE0S", "COMB0",
              "T1", "ATT1A", "ATT1B", "FFN1"]
CAP = T // 4
GROUPS = [ALL_PHASES[:10], ALL_PHASES[10:]]

ENGINES = ("tensor", "vector", "scalar", "gpsimd", "sync")
_DMA_SEM = {"sync": "dy", "gpsimd": "dg", "scalar": "ds", "vector": "dv", "tensor": None}
_CMP_SEM = {"tensor": "t", "vector": "v", "scalar": "s", "gpsimd": "g", "sync": "y"}


class Sched:
    def __init__(self, nc, ctx):
        self.nc = nc
        self.sems = {}
        for name in ("t", "v", "s", "g", "y", "dy", "dg", "ds", "dv"):
            self.sems[name] = ctx.enter_context(nc.semaphore("sem_" + name))
        self.cnt = {k: 0 for k in self.sems}
        self.streams = {e: [] for e in ENGINES}
        self.last_w = {}
        self.readers = {}
        self.clock = {e: {k: 0 for k in self.sems} for e in ENGINES}

    def bc_reg(self, g, val):
        r = getattr(self, "_bcreg", None)
        if r is None:
            r = g.to_reg(val)
            self._bcreg = r
            self._bcval = val
        else:
            assert self._bcval == val
        return r

    def barrier(self):
        self.floor = dict(self.cnt)

    def op(self, eng, fn, reads=(), writes=(), dma=False):
        sem = _DMA_SEM[eng] if dma else _CMP_SEM[eng]
        assert sem is not None, f"engine {eng} cannot dma={dma}"
        eclk = self.clock[eng]
        waits = {}
        floor = getattr(self, "floor", None)
        if floor:
            for s2, v2 in floor.items():
                if v2 > 0 and eclk.get(s2, 0) < v2:
                    waits[s2] = max(waits.get(s2, 0), v2)

        def need(dep):
            dsem, dcnt, dclk = dep
            if eng == "tensor" and dsem == "t":
                pass_wait = True
            else:
                pass_wait = False
            if eclk.get(dsem, 0) >= dcnt:
                return
            if pass_wait:
                for k2, v2 in dclk.items():
                    if eclk[k2] < v2:
                        eclk[k2] = v2
                return
            waits[dsem] = max(waits.get(dsem, 0), dcnt)
            for k2, v2 in dclk.items():
                if eclk[k2] < v2:
                    eclk[k2] = v2

        for k in reads:
            if k in self.last_w:
                need(self.last_w[k])
        for k in writes:
            if k in self.last_w:
                need(self.last_w[k])
            for r in self.readers.get(k, ()):
                need(r)
        for s2, v2 in waits.items():
            if eclk[s2] < v2:
                eclk[s2] = v2
        inc = 16 if dma else 1
        self.cnt[sem] += inc
        cnt_after = self.cnt[sem]
        rec_clk = dict(eclk)
        rec_clk[sem] = max(rec_clk.get(sem, 0), cnt_after)
        rec = (sem, cnt_after, rec_clk)
        for k in writes:
            self.last_w[k] = rec
            self.readers[k] = []
        for k in reads:
            self.readers.setdefault(k, []).append(rec)
        st = self.streams[eng]
        for s2, v2 in sorted(waits.items()):
            st.append(("w", s2, v2))
        st.append(("op", fn, sem, inc))

    def dma(self, eng, out_ap, in_ap, reads=(), writes=(), **kw):
        self.op(eng, lambda e: e.dma_start(out=out_ap, in_=in_ap, **kw),
                reads=reads, writes=writes, dma=True)

    def dmat(self, eng, out_ap, in_ap, reads=(), writes=()):
        self.op(eng, lambda e: e.dma_start_transpose(out_ap, in_ap),
                reads=reads, writes=writes, dma=True)

    def finalize(self, blk):
        st = self.streams["sync"]
        for s2 in sorted(self.sems):
            if self.cnt[s2] > 0 and s2 != "y":
                st.append(("w", s2, self.cnt[s2]))
        for eng in ENGINES:
            items = self.streams[eng]
            if not items:
                continue

            def run(e, items=items):
                for it in items:
                    if it[0] == "w":
                        e.wait_ge(self.sems[it[1]], it[2])
                    else:
                        ins = it[1](e)
                        ins.then_inc(self.sems[it[2]], it[3])
            getattr(blk, eng)(run)



def mm_group(S, psum_ap, pairs, reads, psum_key):
    n = len(pairs)

    def fn(t):
        last = None
        for i, (l, r) in enumerate(pairs):
            last = t.matmul(psum_ap, l, r, start=(i == 0), stop=(i == n - 1))
        return last
    S.op("tensor", fn, reads=list(reads), writes=[psum_key])


def mm_acc(S, psum_ap, pairs, reads, psum_key, start, stop):
    n = len(pairs)

    def fn(t):
        last = None
        for i, (l, r) in enumerate(pairs):
            last = t.matmul(psum_ap, l, r, start=(start and i == 0),
                            stop=(stop and i == n - 1))
        return last
    rw = [psum_key]
    S.op("tensor", fn, reads=list(reads) + ([] if start else [psum_key]), writes=rw)


def emit_ln_tile(S, E, x_f32_ap, x_key, out_bf_ap, out_key, tmp, also_router=None):
    sq, s1, s2, m, ms, var, rstd, nmr = (tmp[k] for k in
                                         ("sq", "s1", "s2", "m", "ms", "var", "rstd", "nmr"))
    kq, k1, k2, km, kms, kv, kr, kn = (tmp[k + "_key"] for k in
                                       ("sq", "s1", "s2", "m", "ms", "var", "rstd", "nmr"))
    S.op("vector", lambda v: v.tensor_mul(sq[:], x_f32_ap, x_f32_ap), reads=[x_key], writes=[kq])
    S.op("vector", lambda v: v.reduce_sum(s1[:], x_f32_ap, axis=AX.X), reads=[x_key], writes=[k1])
    S.op("vector", lambda v: v.reduce_sum(s2[:], sq[:], axis=AX.X), reads=[kq], writes=[k2])
    S.op("vector", lambda v: v.tensor_scalar(m[:], s1[:], 1.0 / D, None, ALU.mult),
         reads=[k1], writes=[km])
    S.op("vector", lambda v: v.tensor_mul(ms[:], m[:], m[:]), reads=[km], writes=[kms])
    S.op("vector", lambda v: v.tensor_scalar(var[:], s2[:], 1.0 / D, EPS, ALU.mult, ALU.add),
         reads=[k2], writes=[kv])
    S.op("vector", lambda v: v.tensor_sub(var[:], var[:], ms[:]), reads=[kv, kms], writes=[kv])
    S.op("scalar", lambda s: s.activation(rstd[:], var[:], AF.Ln), reads=[kv], writes=[kr])
    S.op("scalar", lambda s: s.activation(rstd[:], rstd[:], AF.Exp, scale=-0.5),
         reads=[kr], writes=[kr])
    S.op("vector", lambda v: v.tensor_mul(nmr[:], m[:], rstd[:]), reads=[km, kr], writes=[kn])
    S.op("vector", lambda v: v.tensor_scalar(nmr[:], nmr[:], -1.0, None, ALU.mult),
         reads=[kn], writes=[kn])
    S.op("scalar", lambda s: s.activation(out_bf_ap, x_f32_ap, AF.Identity,
                                          bias=nmr[:, 0:1], scale=rstd[:, 0:1]),
         reads=[x_key, kn, kr], writes=[out_key])



def ph_ln_generic(S, E, src_name, dst_name, with_router):
    nc = S.nc
    ctx = ExitStack()
    pre = dst_name + "_"
    xt = [ctx.enter_context(nc.sbuf_tensor(pre + f"xt{i}", [128, D], F32)) for i in range(2)]
    ot = [ctx.enter_context(nc.sbuf_tensor(pre + f"ot{i}", [128, D], BF16)) for i in range(2)]
    tmp = {}
    for nm, w in (("sq", D), ("s1", 1), ("s2", 1), ("m", 1), ("ms", 1),
                  ("var", 1), ("rstd", 1), ("nmr", 1)):
        tmp[nm] = ctx.enter_context(nc.sbuf_tensor(pre + nm, [128, w], F32))
        tmp[nm + "_key"] = pre + nm
    src = E[src_name]
    dst = E[dst_name]
    if with_router:
        gw_f = ctx.enter_context(nc.sbuf_tensor(pre + "gw", [128, 8, 8], F32))
        idf = E["sb_ident_f32"]
        xTf = ctx.enter_context(nc.sbuf_tensor(pre + "xTf", [128, 8, 128], F32))
        e8 = ctx.enter_context(nc.sbuf_tensor(pre + "e8", [128, 8], F32))
        rmaxn = ctx.enter_context(nc.sbuf_tensor(pre + "rmaxn", [128, 1], F32))
        msk = ctx.enter_context(nc.sbuf_tensor(pre + "msk", [128, 8], F32))
        den = ctx.enter_context(nc.sbuf_tensor(pre + "den", [128, 1], F32))
        L_s = ctx.enter_context(nc.sbuf_tensor(pre + "Ltri", [128, 128], F32))
        run_s = ctx.enter_context(nc.sbuf_tensor(pre + "run", [1, 8], F32))
        on1 = ctx.enter_context(nc.sbuf_tensor(pre + "on1", [1, 128], F32))
        onc = ctx.enter_context(nc.sbuf_tensor(pre + "onc", [128, 1], F32))
        t8 = ctx.enter_context(nc.sbuf_tensor(pre + "t8", [128, 8], F32))
        sidx_f = ctx.enter_context(nc.sbuf_tensor(pre + "sidxf", [128, 1], F32))
        wc_sb = ctx.enter_context(nc.sbuf_tensor(pre + "wcsb", [128, NT128], F32))
        si_sb = ctx.enter_context(nc.sbuf_tensor(pre + "sisb", [128, NT128], mybir.dt.int32))
        PS_T = ctx.enter_context(nc.psum_tensor(pre + "pst", [128, 512], F32))
        PS_L = ctx.enter_context(nc.psum_tensor(pre + "psl", [128, 512], F32))
        PS_G = ctx.enter_context(nc.psum_tensor(pre + "psg", [128, 512], F32))
        kT, kL, kG = pre + "pst", pre + "psl", pre + "psg"
    with ctx:
        if with_router:
            S.dma("sync", gw_f[:], E["gate_w"].rearrange("(k p) e -> p k e", p=128),
                  writes=["sb:gw"])
            S.dma("sync", L_s[:], E["Ltri"], writes=[pre + "L"])
            S.dma("sync", run_s[:], E["eoff"], writes=[pre + "run"])
            S.op("vector", lambda v: v.memset(on1[:], 1.0), writes=[pre + "on1"])
            S.op("vector", lambda v: v.memset(onc[:], 1.0), writes=[pre + "onc"])
        for i in range(NT128):
            bi = i % 2
            xk, ok = f"{pre}xt{bi}", f"{pre}ot{bi}"
            S.dma("sync", xt[bi][:], src[i * 128:(i + 1) * 128, :], writes=[xk])
            emit_ln_tile(S, E, xt[bi][:], xk, ot[bi][:], ok, tmp)
            S.dma("sync", dst[i * 128:(i + 1) * 128, :], ot[bi][:], reads=[ok], writes=[dst_name])
            if with_router:
                xnf = tmp["sq"]
                S.op("scalar", lambda s, bi=bi: s.activation(
                    xnf[:], xt[bi][:], AF.Identity, bias=tmp["nmr"][:, 0:1],
                    scale=tmp["rstd"][:, 0:1]),
                    reads=[xk, pre + "nmr", pre + "rstd"], writes=[pre + "sq"])
                for dt in range(8):
                    S.op("tensor", lambda t, dt=dt: t.transpose(
                        PS_T[:, 0:128], xnf[:, dt * 128:(dt + 1) * 128], idf[:]),
                        reads=[pre + "sq", "sb:identf"], writes=[kT])
                    S.op("scalar", lambda s, dt=dt: s.copy(xTf[:, dt, :], PS_T[:, 0:128]),
                         reads=[kT], writes=[f"{pre}xTf{dt}"])
                pairs = [(xTf[:, dt, :], gw_f[:, dt, :]) for dt in range(8)]
                mm_group(S, PS_L[:, 0:8], pairs,
                         [f"{pre}xTf{dt}" for dt in range(8)] + ["sb:gw"], kL)
                S.op("vector", lambda v: v.tensor_reduce(rmaxn[:], PS_L[:, 0:8], AX.X,
                                                         ALU.max, negate=True),
                     reads=[kL], writes=[pre + "rmaxn"])
                S.op("vector", lambda v: v.tensor_scalar(msk[:], PS_L[:, 0:8],
                                                         rmaxn[:, 0:1], 0.0, ALU.add, ALU.is_ge),
                     reads=[kL, pre + "rmaxn"], writes=[pre + "msk"])
                S.op("scalar", lambda s: s.activation(e8[:], PS_L[:, 0:8], AF.Exp,
                                                      bias=rmaxn[:, 0:1]),
                     reads=[kL, pre + "rmaxn"], writes=[pre + "e8"])
                S.op("vector", lambda v: v.reduce_sum(den[:], e8[:], axis=AX.X),
                     reads=[pre + "e8"], writes=[pre + "den"])
                S.op("vector", lambda v, i=i: v.reciprocal(wc_sb[:, i:i + 1], den[:]),
                     reads=[pre + "den"], writes=[pre + "wcsb"])
                mm_group(S, PS_G[:, 0:8], [(L_s[:], msk[:]), (on1[0:1, :], run_s[0:1, :])],
                         [pre + "L", pre + "msk", pre + "on1", pre + "run"], kG)
                S.op("vector", lambda v: v.tensor_mul(t8[:], msk[:], PS_G[:, 0:8]),
                     reads=[pre + "msk", kG], writes=[pre + "t8"])
                S.op("vector", lambda v: v.reduce_sum(sidx_f[:], t8[:], axis=AX.X),
                     reads=[pre + "t8"], writes=[pre + "sidxf"])
                S.op("vector", lambda v, i=i: v.tensor_copy(si_sb[:, i:i + 1], sidx_f[:]),
                     reads=[pre + "sidxf"], writes=[pre + "sisb"])
                mm_group(S, PS_G[0:1, 8:16], [(onc[:, 0:1], msk[:])],
                         [pre + "onc", pre + "msk"], kG)
                S.op("vector", lambda v: v.tensor_tensor(run_s[0:1, :], run_s[0:1, :],
                                                         PS_G[0:1, 8:16], ALU.add),
                     reads=[pre + "run", kG], writes=[pre + "run"])
                S.op("gpsimd", lambda g, i=i, bi=bi: g.indirect_dma_start(
                    E["xg"], bass.IndirectOffsetOnAxis(si_sb[:, i:i + 1], 0),
                    ot[bi][:], None, bounds_check=S.bc_reg(g, 2 * T), oob_is_err=False),
                    reads=[pre + "sisb", f"{pre}ot{bi}"], writes=["xg"], dma=True)
        if with_router:
            S.dma("sync", E["wcol"], wc_sb[:], reads=[pre + "wcsb"], writes=["wcol"])
            S.dma("sync", E["sidx"], si_sb[:], reads=[pre + "sisb"], writes=["sidx"])
    S.barrier()


def ph_transpose(S, E, src_name, dst_name):
    nc = S.nc
    ctx = ExitStack()
    pre = dst_name + "_"
    tt = [ctx.enter_context(nc.sbuf_tensor(pre + f"tt{i}", [128, T], BF16)) for i in range(2)]
    src, dst = E[src_name], E[dst_name]
    with ctx:
        for dt in range(8):
            bi = dt % 2
            k = f"{pre}tt{bi}"
            S.dmat("sync", tt[bi][:], src[0:T, dt * 128:(dt + 1) * 128],
                   reads=[src_name], writes=[k])
            S.dma("sync", dst[dt * 128:(dt + 1) * 128, :], tt[bi][:], reads=[k],
                  writes=[dst_name])
    S.barrier()


def build(group, dbg_all_outputs=False):
    nc = bass.Bass("TRN2", target_bir_lowering=False)
    ctx = ExitStack()
    E = {"ctx": ctx}

    used_inputs = []
    for ph in group:
        for nm in PHASES[ph][0]:
            if nm not in used_inputs:
                used_inputs.append(nm)
    reads = set()
    writes = set()
    for ph in group:
        reads.update(PHASES[ph][1])
        writes.update(PHASES[ph][2])
    ext_in, ext_out = [], []
    ISHAPES = host_input_shapes()
    for nm in used_inputs:
        shp, dt = ISHAPES[nm]
        E[nm] = nc.dram_tensor(nm, list(shp), dt, kind="ExternalInput").ap()
        ext_in.append(nm)
    for nm in sorted(reads | writes):
        if nm == "y":
            continue
        base = nm[:-4] if nm.endswith("_sib") else nm
        shp, dt = SCRATCH[base]
        if nm in writes:
            later = any(nm in PHASES[p][1] for g in GROUPS for p in g if p not in group)
            kind = "ExternalOutput" if (later or dbg_all_outputs) else "Internal"
            if nm in reads and kind == "ExternalOutput":
                kind = "ExternalOutput"
        else:
            kind = "ExternalInput"
        E[nm] = nc.dram_tensor(nm, list(shp), dt, kind=kind).ap()
        if kind == "ExternalInput":
            ext_in.append(nm)
        elif kind == "ExternalOutput":
            ext_out.append(nm)
    if any("y" in PHASES[p][2] for p in group):
        E["y"] = nc.dram_tensor("y", OUTPUT[1], OUTPUT[2], kind="ExternalOutput").ap()
        ext_out.append("y")

    with ctx:
        if "ident_f32" in E:
            E["sb_ident_f32"] = ctx.enter_context(nc.sbuf_tensor("identf", [128, 128], F32))
        S = Sched(nc, ctx)
        if "ident_f32" in E:
            S.dma("sync", E["sb_ident_f32"][:], E["ident_f32"], writes=["sb:identf"])

        for ph in group:
            if ph == "LN0":
                ph_ln_generic(S, E, "x_main", "xn0", with_router=True)
            elif ph == "PRELN":
                ph_ln_generic(S, E, "x_pre", "xnp", with_router=False)
            elif ph == "T0":
                ph_transpose(S, E, "xn0", "xn0T")
            elif ph == "PRET":
                ph_transpose(S, E, "xnp", "xnpT")
            elif ph == "PRE0":
                ph_pre0(S, E)
            elif ph == "ATT0":
                ph_att0(S, E)
            elif ph == "MOE0S":
                ph_moe0s(S, E)
            elif ph == "COMB0":
                ph_comb0(S, E)
            elif ph == "T1":
                ph_transpose(S, E, "xn1", "xn1T")
            elif ph == "ATT1A":
                ph_att1a(S, E)
            elif ph == "ATT1B":
                ph_att1b(S, E)
            elif ph == "FFN1":
                ph_ffn1(S, E)
            else:
                raise ValueError(ph)

        blk = ctx.enter_context(nc.Block())
        S.finalize(blk)
    ext_out += E.get("extra_out", [])
    return nc, ext_in, ext_out


GAMMA = [float(1.0 - 2.0 ** (-5 - h)) for h in range(NH)]
GAMC = [float(g ** CH) for g in GAMMA]


def _rope_rowmajor(S, E, ps_ap_of, kr, cos_ap, sin_ap, tkeys, out_ap_of, okey,
                   t1ap, t2ap, k1, k2):
    for h in range(NH):
        x0, x1 = ps_ap_of(h)
        o0, o1 = out_ap_of(h)
        S.op("vector", lambda v, x0=x0: v.tensor_mul(t1ap, x0, cos_ap),
             reads=[kr(h)] + tkeys, writes=[k1])
        S.op("vector", lambda v, x1=x1: v.tensor_mul(t2ap, x1, sin_ap),
             reads=[kr(h)] + tkeys, writes=[k2])
        S.op("vector", lambda v, o0=o0: v.tensor_sub(o0, t1ap, t2ap),
             reads=[k1, k2], writes=[okey(h)])
        S.op("vector", lambda v, x1=x1: v.tensor_mul(t1ap, x1, cos_ap),
             reads=[kr(h)] + tkeys, writes=[k1])
        S.op("vector", lambda v, x0=x0: v.tensor_mul(t2ap, x0, sin_ap),
             reads=[kr(h)] + tkeys, writes=[k2])
        S.op("vector", lambda v, o1=o1: v.tensor_add(o1, t1ap, t2ap),
             reads=[k1, k2], writes=[okey(h)])


def ph_pre0(S, E):
    nc, ctx = S.nc, E["ctx"]
    pctx = ExitStack()
    ps = [pctx.enter_context(nc.psum_tensor(f"pre_ps{i}", [128, 512], F32)) for i in range(8)]
    wk_s = pctx.enter_context(nc.sbuf_tensor("pre_wk", [128, 8, D], BF16))
    wv_s = pctx.enter_context(nc.sbuf_tensor("pre_wv", [128, 8, VDIM], BF16))
    cos_s = pctx.enter_context(nc.sbuf_tensor("pre_cos", [128, NCH, 64], BF16))
    sin_s = pctx.enter_context(nc.sbuf_tensor("pre_sin", [128, NCH, 64], BF16))
    wdec_s = pctx.enter_context(nc.sbuf_tensor("pre_wdec", [128, NH], F32))
    xc = [pctx.enter_context(nc.sbuf_tensor(f"pre_xc{i}", [128, 8, CH], BF16)) for i in range(2)]
    krw = pctx.enter_context(nc.sbuf_tensor("pre_krw", [128, D], BF16))
    vsb = pctx.enter_context(nc.sbuf_tensor("pre_v", [128, VDIM], BF16))
    t1 = pctx.enter_context(nc.sbuf_tensor("pre_t1", [128, 64], F32))
    t2 = pctx.enter_context(nc.sbuf_tensor("pre_t2", [128, 64], F32))
    Ssb = pctx.enter_context(nc.sbuf_tensor("pre_S", [128, NH * DV], F32))
    with pctx:
        S.dma("gpsimd", wk_s[:], E["wk"].rearrange("(k p) f -> p k f", p=128), writes=["pre:wk"])
        S.dma("gpsimd", wv_s[:], E["wv"].rearrange("(k p) f -> p k f", p=128), writes=["pre:wv"])
        S.dma("sync", cos_s[:], E["cosp"].rearrange("(c p) f -> p c f", p=128), writes=["pre:tab"])
        S.dma("sync", sin_s[:], E["sinp"].rearrange("(c p) f -> p c f", p=128), writes=["pre:tab"])
        S.dma("sync", wdec_s[:], E["wdec"], writes=["pre:wdec"])
        S.op("vector", lambda v: v.memset(Ssb[:], 0.0), writes=["pre:S"])
        for ci in range(NCH):
            bi = ci % 2
            xk = f"pre:xc{bi}"
            S.dma("sync", xc[bi][:], E["xnpT"][:, ci * CH:(ci + 1) * CH]
                  .rearrange("(d p) t -> p d t", p=128), reads=["xnpT"], writes=[xk])
            for half in range(2):
                pairs = [(xc[bi][:, dt, :], wk_s[:, dt, half * 512:(half + 1) * 512])
                         for dt in range(8)]
                mm_group(S, ps[half][:], pairs, [xk, "pre:wk"], f"pre_ps{half}")
            cos_ap = cos_s[:, ci, :]
            sin_ap = sin_s[:, ci, :]

            def psof(h):
                bank = ps[h // 4]
                off = (h % 4) * 128
                return bank[:, off:off + 64], bank[:, off + 64:off + 128]

            def outof(h):
                return krw[:, h * 128:h * 128 + 64], krw[:, h * 128 + 64:(h + 1) * 128]
            _rope_rowmajor(S, E, psof, lambda h: f"pre_ps{h // 4}", cos_ap, sin_ap,
                           ["pre:tab"], outof, lambda h: "pre:krw", t1[:], t2[:],
                           "pre:t1", "pre:t2")
            for h in range(NH):
                S.op("vector", lambda v, h=h: v.tensor_scalar(
                    krw[:, h * 128:(h + 1) * 128], krw[:, h * 128:(h + 1) * 128],
                    wdec_s[:, h:h + 1], None, ALU.mult),
                    reads=["pre:krw", "pre:wdec"], writes=["pre:krw"])
            for half in range(4):
                pairs = [(xc[bi][:, dt, :], wv_s[:, dt, half * 512:(half + 1) * 512])
                         for dt in range(8)]
                mm_group(S, ps[2 + half][:], pairs, [xk, "pre:wv"], f"pre_ps{2 + half}")
            for half in range(4):
                S.op("scalar", lambda s, half=half: s.copy(
                    vsb[:, half * 512:(half + 1) * 512], ps[2 + half][:]),
                    reads=[f"pre_ps{2 + half}"], writes=["pre:v"])
            for h in range(NH):
                bank = 6 + h % 2
                mm_group(S, ps[bank][:, 0:256],
                         [(krw[:, h * 128:(h + 1) * 128], vsb[:, h * 256:(h + 1) * 256])],
                         ["pre:krw", "pre:v"], f"pre_ps{bank}")
                S.op("vector", lambda v, h=h, bank=bank: v.scalar_tensor_tensor(
                    Ssb[:, h * 256:(h + 1) * 256], Ssb[:, h * 256:(h + 1) * 256],
                    GAMC[h], ps[bank][:, 0:256], ALU.mult, ALU.add),
                    reads=["pre:S", f"pre_ps{bank}"], writes=["pre:S"])
        S.dma("sync", E["S_pre"], Ssb[:], reads=["pre:S"], writes=["S_pre"])
    S.barrier()


DBG_ATT0 = False


def ph_att0(S, E):
    nc = S.nc
    if DBG_ATT0:
        for nm, shp, dt in (("dbg_qps", [128, 128], F32), ("dbg_t1", [64, 128], F32),
                            ("dbg_qTr", [128, NH, 128], BF16), ("dbg_kTr", [128, NH, 128], BF16),
                            ("dbg_Kr", [128, D], BF16), ("dbg_V", [128, VDIM], BF16),
                            ("dbg_o", [128, NH, DV], BF16), ("dbg_gsil", [128, VDIM], BF16),
                            ("dbg_og", [128, NH, DV], BF16)):
            E[nm] = nc.dram_tensor(nm, shp, dt, kind="ExternalOutput").ap()
            E.setdefault("extra_out", []).append(nm)
    pctx = ExitStack()
    ps = [pctx.enter_context(nc.psum_tensor(f"a0_ps{i}", [128, 512], F32)) for i in range(6)]
    pb = [pctx.enter_context(nc.psum_tensor(f"a0_pb{i}", [128, 1024], BF16)) for i in range(2)]
    sb = lambda n, shp, dt=BF16: pctx.enter_context(nc.sbuf_tensor("a0_" + n, shp, dt))
    wq_s = sb("wq", [128, 8, D])
    wk_s = sb("wk", [128, 8, D])
    wv_s = sb("wv", [128, 8, VDIM])
    wg_s = sb("wg", [128, 8, VDIM])
    wo_s = sb("wo", [128, 16, D])
    cos_s = sb("cos", [128, NCH, 64])
    sin_s = sb("sin", [128, NCH, 64])
    cosT_s = sb("cosT", [64, T])
    sinT_s = sb("sinT", [64, T])
    dmask_s = sb("dmask", [128, NH, 128])
    gpow_s = sb("gpow", [128, NH, 128])
    wdec_s = sb("wdec", [128, NH], F32)
    smask_s = sb("smask", [128, 1], F32)
    gnw_s = sb("gnw", [128, DV], F32)
    idb = sb("idb", [128, 128])
    xc = [sb(f"xc{i}", [128, 8, CH]) for i in range(2)]
    qTr = sb("qTr", [128, NH, 128])
    qTs = sb("qTs", [128, NH, 128])
    kTr = sb("kTr", [128, NH, 128])
    Kr = sb("Kr", [128, D])
    krw = sb("krw", [128, 128])
    Vsb = sb("V", [128, VDIM])
    Asb = sb("As", [128, 128])
    osb = sb("o", [128, NH, DV])
    gsil = sb("gsil", [128, VDIM])
    ogT = sb("ogT", [128, 16, 128])
    a0t = [sb(f"a0t{i}", [128, D]) for i in range(2)]
    Ssb = sb("S", [128, NH * DV], F32)
    Sbf = sb("Sbf", [128, NH * DV])
    t1 = sb("t1", [128, 128], F32)
    t2 = sb("t2", [128, 128], F32)
    rms = sb("rms", [128, NH], F32)
    with pctx:
        S.dma("gpsimd", wq_s[:], E["wq"].rearrange("(k p) f -> p k f", p=128), writes=["a0:wq"])
        S.dma("gpsimd", wk_s[:], E["wk"].rearrange("(k p) f -> p k f", p=128), writes=["a0:wk"])
        S.dma("gpsimd", wv_s[:], E["wv"].rearrange("(k p) f -> p k f", p=128), writes=["a0:wv"])
        S.dma("gpsimd", wg_s[:], E["wg"].rearrange("(k p) f -> p k f", p=128), writes=["a0:wg"])
        S.dma("gpsimd", wo_s[:], E["wo"].rearrange("(k p) f -> p k f", p=128), writes=["a0:wo"])
        S.dma("sync", cos_s[:], E["cosm"].rearrange("(c p) f -> p c f", p=128), writes=["a0:tab"])
        S.dma("sync", sin_s[:], E["sinm"].rearrange("(c p) f -> p c f", p=128), writes=["a0:tab"])
        S.dma("sync", cosT_s[:], E["cosTm"], writes=["a0:tabT"])
        S.dma("sync", sinT_s[:], E["sinTm"], writes=["a0:tabT"])
        S.dma("sync", dmask_s[:], E["dmask"].rearrange("h j i -> j h i"), writes=["a0:dm"])
        S.dma("sync", gpow_s[:], E["gpow"].rearrange("h d i -> d h i"), writes=["a0:gp"])
        S.dma("sync", wdec_s[:], E["wdec"], writes=["a0:wdec"])
        S.dma("sync", smask_s[:], E["smask"], writes=["a0:smask"])
        S.dma("sync", gnw_s[:], E["gnwb"], writes=["a0:gnw"])
        S.dma("gpsimd", idb[:], E["ident_f32"], writes=["a0:idb"])
        S.dma("sync", Ssb[:], E["S_pre"], reads=["S_pre"], writes=["a0:S"])
        S.op("vector", lambda v: v.tensor_scalar(Ssb[:], Ssb[:], smask_s[:, 0:1], None, ALU.mult),
             reads=["a0:S", "a0:smask"], writes=["a0:S"])
        S.op("vector", lambda v: v.tensor_copy(Sbf[:], Ssb[:]), reads=["a0:S"], writes=["a0:Sbf"])
        for ci in range(NCH):
            bi = ci % 2
            xk = f"a0:xc{bi}"
            S.dma("sync", xc[bi][:], E["xn0T"][:, ci * CH:(ci + 1) * CH]
                  .rearrange("(d p) t -> p d t", p=128), reads=["xn0T"], writes=[xk])
            ctk = ["a0:tabT"]
            csl = cosT_s[:, ci * CH:(ci + 1) * CH]
            ssl = sinT_s[:, ci * CH:(ci + 1) * CH]
            for h in range(NH):
                for (wsb, wkey, bank, out, okey) in (
                        (wq_s, "a0:wq", 0, qTr, "a0:qTr"), (wk_s, "a0:wk", 1, kTr, "a0:kTr")):
                    pairs = [(wsb[:, dt, h * 128:(h + 1) * 128], xc[bi][:, dt, :])
                             for dt in range(8)]
                    mm_group(S, ps[bank][:, 0:128], pairs, [xk, wkey], f"a0_ps{bank}")
                    pk = f"a0_ps{bank}"
                    if DBG_ATT0 and ci == 0 and h == 3 and bank == 0:
                        S.op("scalar", lambda s: s.copy(t1[:], ps[0][:, 0:128]),
                             reads=["a0_ps0"], writes=["a0:t1"])
                        S.dma("sync", E["dbg_qps"], t1[:], reads=["a0:t1"], writes=["dbg_qps"])
                    x0, x1 = ps[bank][0:64, 0:128], ps[bank][64:128, 0:128]
                    o0 = out[0:64, h, :]
                    o1 = out[64:128, h, :]
                    S.op("vector", lambda v, x0=x0, csl=csl: v.tensor_mul(t1[0:64, :], x0, csl),
                         reads=[pk] + ctk, writes=["a0:t1"])
                    S.op("vector", lambda v, x1=x1, ssl=ssl: v.tensor_mul(t2[0:64, :], x1, ssl),
                         reads=[pk] + ctk, writes=["a0:t2"])
                    S.op("vector", lambda v, o0=o0: v.tensor_sub(o0, t1[0:64, :], t2[0:64, :]),
                         reads=["a0:t1", "a0:t2"], writes=[okey])
                    S.op("vector", lambda v, x1=x1, csl=csl: v.tensor_mul(t1[0:64, :], x1, csl),
                         reads=[pk] + ctk, writes=["a0:t1"])
                    S.op("vector", lambda v, x0=x0, ssl=ssl: v.tensor_mul(t2[0:64, :], x0, ssl),
                         reads=[pk] + ctk, writes=["a0:t2"])
                    S.op("vector", lambda v, o1=o1: v.tensor_add(o1, t1[0:64, :], t2[0:64, :]),
                         reads=["a0:t1", "a0:t2"], writes=[okey])
                S.op("vector", lambda v, h=h: v.tensor_mul(qTs[:, h, :], qTr[:, h, :],
                                                           gpow_s[:, h, :]),
                     reads=["a0:qTr", "a0:gp"], writes=["a0:qTs"])
            for half in range(2):
                pairs = [(xc[bi][:, dt, :], wk_s[:, dt, half * 512:(half + 1) * 512])
                         for dt in range(8)]
                mm_group(S, ps[2 + half][:], pairs, [xk, "a0:wk"], f"a0_ps{2 + half}")
            cos_ap = cos_s[:, ci, :]
            sin_ap = sin_s[:, ci, :]

            def psof(h):
                bank = ps[2 + h // 4]
                off = (h % 4) * 128
                return bank[:, off:off + 64], bank[:, off + 64:off + 128]

            def outof(h):
                return Kr[:, h * 128:h * 128 + 64], Kr[:, h * 128 + 64:(h + 1) * 128]
            _rope_rowmajor(S, E, psof, lambda h: f"a0_ps{2 + h // 4}", cos_ap, sin_ap,
                           ["a0:tab"], outof, lambda h: "a0:Kr", t1[:, 0:64], t2[:, 0:64],
                           "a0:t1", "a0:t2")
            for half in range(4):
                bank = 2 + half % 2
                pairs = [(xc[bi][:, dt, :], wv_s[:, dt, half * 512:(half + 1) * 512])
                         for dt in range(8)]
                mm_group(S, ps[bank][:], pairs, [xk, "a0:wv"], f"a0_ps{bank}")
                S.op("scalar", lambda s, half=half, bank=bank: s.copy(
                    Vsb[:, half * 512:(half + 1) * 512], ps[bank][:]),
                    reads=[f"a0_ps{bank}"], writes=["a0:V"])
            for h in range(NH):
                mm_group(S, ps[4][:, 0:128], [(kTr[:, h, :], qTr[:, h, :])],
                         ["a0:kTr", "a0:qTr"], "a0_ps4")
                S.op("vector", lambda v, h=h: v.tensor_mul(Asb[:], ps[4][:, 0:128],
                                                           dmask_s[:, h, :]),
                     reads=["a0_ps4", "a0:dm"], writes=["a0:As"])
                mm_group(S, ps[5][:, 0:256],
                         [(Asb[:], Vsb[:, h * 256:(h + 1) * 256]),
                          (qTs[:, h, :], Sbf[:, h * 256:(h + 1) * 256])],
                         ["a0:As", "a0:V", "a0:qTs", "a0:Sbf"], "a0_ps5")
                S.op("scalar", lambda s, h=h: s.copy(osb[:, h, :], ps[5][:, 0:256]),
                     reads=["a0_ps5"], writes=["a0:o"])
                S.op("vector", lambda v, h=h: v.tensor_scalar(
                    krw[:], Kr[:, h * 128:(h + 1) * 128], wdec_s[:, h:h + 1], None, ALU.mult),
                    reads=["a0:Kr", "a0:wdec"], writes=["a0:krw"])
                mm_group(S, ps[4][:, 0:256], [(krw[:], Vsb[:, h * 256:(h + 1) * 256])],
                         ["a0:krw", "a0:V"], "a0_ps4")
                S.op("vector", lambda v, h=h: v.scalar_tensor_tensor(
                    Ssb[:, h * 256:(h + 1) * 256], Ssb[:, h * 256:(h + 1) * 256],
                    GAMC[h], ps[4][:, 0:256], ALU.mult, ALU.add),
                    reads=["a0:S", "a0_ps4"], writes=["a0:S"])
                S.op("vector", lambda v, h=h: v.tensor_copy(Sbf[:, h * 256:(h + 1) * 256],
                                                            Ssb[:, h * 256:(h + 1) * 256]),
                     reads=["a0:S"], writes=["a0:Sbf"])
            if DBG_ATT0 and ci == 0:
                S.dma("sync", E["dbg_qTr"], qTr[:], reads=["a0:qTr"], writes=["dbg_qTr"])
                S.dma("sync", E["dbg_kTr"], kTr[:], reads=["a0:kTr"], writes=["dbg_kTr"])
                S.dma("sync", E["dbg_Kr"], Kr[:], reads=["a0:Kr"], writes=["dbg_Kr"])
                S.dma("sync", E["dbg_V"], Vsb[:], reads=["a0:V"], writes=["dbg_V"])
                S.dma("sync", E["dbg_o"], osb[:], reads=["a0:o"], writes=["dbg_o"])
            for half in range(4):
                bank = 2 + half % 2
                pairs = [(xc[bi][:, dt, :], wg_s[:, dt, half * 512:(half + 1) * 512])
                         for dt in range(8)]
                mm_group(S, ps[bank][:], pairs, [xk, "a0:wg"], f"a0_ps{bank}")
                S.op("scalar", lambda s, half=half, bank=bank: s.activation(
                    gsil[:, half * 512:(half + 1) * 512], ps[bank][:], AF.Silu),
                    reads=[f"a0_ps{bank}"], writes=["a0:gsil"])
            for h in range(NH):
                S.op("vector", lambda v, h=h: v.tensor_mul(t1[:], osb[:, h, 0:128],
                                                           osb[:, h, 0:128]),
                     reads=["a0:o"], writes=["a0:t1"])
                S.op("vector", lambda v, h=h: v.tensor_mul(t2[:], osb[:, h, 128:256],
                                                           osb[:, h, 128:256]),
                     reads=["a0:o"], writes=["a0:t2"])
                S.op("vector", lambda v, h=h: v.tensor_add(t1[:], t1[:], t2[:]),
                     reads=["a0:t1", "a0:t2"], writes=["a0:t1"])
                S.op("vector", lambda v, h=h: v.reduce_sum(rms[:, h:h + 1], t1[:], axis=AX.X),
                     reads=["a0:t1"], writes=["a0:rms"])
            S.op("vector", lambda v: v.tensor_scalar(rms[:], rms[:], 1.0 / DV, EPS,
                                                     ALU.mult, ALU.add),
                 reads=["a0:rms"], writes=["a0:rms"])
            S.op("scalar", lambda s: s.activation(rms[:], rms[:], AF.Ln),
                 reads=["a0:rms"], writes=["a0:rms"])
            S.op("scalar", lambda s: s.activation(rms[:], rms[:], AF.Exp, scale=-0.5),
                 reads=["a0:rms"], writes=["a0:rms"])
            for h in range(NH):
                S.op("vector", lambda v, h=h: v.tensor_scalar(
                    osb[:, h, :], osb[:, h, :], rms[:, h:h + 1], None, ALU.mult),
                    reads=["a0:o", "a0:rms"], writes=["a0:o"])
                S.op("vector", lambda v, h=h: v.tensor_mul(osb[:, h, :], osb[:, h, :], gnw_s[:]),
                     reads=["a0:o", "a0:gnw"], writes=["a0:o"])
                S.op("vector", lambda v, h=h: v.tensor_mul(
                    osb[:, h, :], osb[:, h, :],
                    gsil[:, h * 256:(h + 1) * 256]),
                    reads=["a0:o", "a0:gsil"], writes=["a0:o"])
            if DBG_ATT0 and ci == 0:
                S.dma("sync", E["dbg_gsil"], gsil[:], reads=["a0:gsil"], writes=["dbg_gsil"])
                S.dma("sync", E["dbg_og"], osb[:], reads=["a0:o"], writes=["dbg_og"])
            for vt in range(16):
                bank = pb[vt % 2]
                S.op("tensor", lambda t, vt=vt, bank=bank: t.transpose(
                    bank[:, 0:128], osb[:].rearrange("p h e -> p (h e)")[:, vt * 128:(vt + 1) * 128],
                    idb[:]),
                    reads=["a0:o", "a0:idb"], writes=[f"a0_pb{vt % 2}"])
                S.op("scalar", lambda s, vt=vt, bank=bank: s.copy(ogT[:, vt, :], bank[:, 0:128]),
                     reads=[f"a0_pb{vt % 2}"], writes=["a0:ogT"])
            bo = bi
            for half in range(2):
                pairs = [(ogT[:, vt, :], wo_s[:, vt, half * 512:(half + 1) * 512])
                         for vt in range(16)]
                mm_group(S, ps[half][:], pairs, ["a0:ogT", "a0:wo"], f"a0_ps{half}")
                S.op("scalar", lambda s, half=half, bo=bo: s.copy(
                    a0t[bo][:, half * 512:(half + 1) * 512], ps[half][:]),
                    reads=[f"a0_ps{half}"], writes=[f"a0:a0t{bo}"])
            S.dma("sync", E["attn0"][ci * CH:(ci + 1) * CH, :], a0t[bo][:],
                  reads=[f"a0:a0t{bo}"], writes=["attn0"])
    S.barrier()


def ph_moe0(S, E):
    nc = S.nc
    TC = 512
    NCK = T // TC
    NQ = 4
    FQ = DFF // NQ
    pctx = ExitStack()
    ps = [pctx.enter_context(nc.psum_tensor(f"mo_ps{i}", [128, 512], F32)) for i in range(6)]
    sb = lambda n, shp, dt=BF16: pctx.enter_context(nc.sbuf_tensor("mo_" + n, shp, dt))
    w1q = [sb(f"w1q{i}", [128, 8, FQ]) for i in range(2)]
    w2q = [sb(f"w2q{i}", [128, 8, D]) for i in range(2)]
    xcs = [sb(f"xc{i}", [128, 8, TC]) for i in range(2)]
    hts = sb("hts", [128, 8, TC])
    b1p_s = sb("b1p", [128, NE * 32], F32)
    b2e_s = sb("b2e", [8, D])
    w8T_s = sb("w8T", [8, T])
    w8row = sb("w8row", [1, T], F32)
    ones1 = sb("ones1", [1, 128], F32)
    ysb = [sb(f"y{i}", [128, 4, D], F32) for i in range(2)]
    with pctx:
        S.dma("sync", b1p_s[:], E["b1p"], writes=["mo:b1p"])
        S.dma("gpsimd", b2e_s[:], E["b2e"], writes=["mo:b2e"])
        S.dma("gpsimd", w8T_s[:], E["w8T"], reads=["w8T"], writes=["mo:w8T"])
        S.dma("sync", w8row[:], E["w8T"][0:1, :], reads=["w8T"], writes=["mo:w8row"])
        S.op("vector", lambda v: v.memset(ones1[:], 1.0), writes=["mo:ones1"])

        def load_q(e, q, ri):
            S.dma("gpsimd", w1q[ri][:],
                  E["w1"][e].rearrange("(k p) f -> p k f", p=128)[:, :, q * FQ:(q + 1) * FQ],
                  writes=[f"mo:w1q{ri}"])
            S.dma("gpsimd", w2q[ri][:],
                  E["w2"][e][q * FQ:(q + 1) * FQ, :].rearrange("(k p) f -> p k f", p=128),
                  writes=[f"mo:w2q{ri}"])
        load_q(0, 0, 0)
        step = 0
        for e in range(NE):
            S.dma("sync", w8row[:], E["w8T"][e:e + 1, :], reads=["w8T"], writes=["mo:w8row"])
            for q in range(NQ):
                ri = step % 2
                ne, nq = (e, q + 1) if q + 1 < NQ else (e + 1, 0)
                if ne < NE:
                    load_q(ne, nq, (step + 1) % 2)
                for ci in range(NCK):
                    xb = ci % 2
                    xk = f"mo:xc{xb}"
                    S.dma("sync", xcs[xb][:], E["xn0T"][:, ci * TC:(ci + 1) * TC]
                          .rearrange("(d p) t -> p d t", p=128), reads=["xn0T"], writes=[xk])
                    mm_group(S, ps[4][:], [(ones1[0:1, :], w8row[0:1, ci * TC:(ci + 1) * TC])],
                             ["mo:ones1", "mo:w8row"], "mo_ps4")
                    for ft in range(8):
                        bank = ft % 2
                        pairs = [(w1q[ri][:, dt, ft * 128:(ft + 1) * 128], xcs[xb][:, dt, :])
                                 for dt in range(8)]
                        mm_group(S, ps[bank][:], pairs, [xk, f"mo:w1q{ri}"], f"mo_ps{bank}")
                        gft = q * 8 + ft
                        S.op("scalar", lambda s, ft=ft, bank=bank, gft=gft, e=e: s.activation(
                            hts[:, ft, :], ps[bank][:], AF.Gelu,
                            bias=b1p_s[:, e * 32 + gft:e * 32 + gft + 1]),
                            reads=[f"mo_ps{bank}", "mo:b1p"], writes=["mo:hts"])
                        S.op("vector", lambda v, ft=ft: v.tensor_tensor(
                            hts[:, ft, :], hts[:, ft, :], ps[4][:], ALU.mult),
                            reads=["mo:hts", "mo_ps4"], writes=["mo:hts"])
                    yb = ci % 2
                    for sub in range(4):
                        for dh in range(2):
                            bank = 2 + dh
                            pairs = []
                            if e == 0 and q == 0:
                                pairs.append((w8T_s[:, ci * TC + sub * 128:ci * TC + (sub + 1) * 128],
                                              b2e_s[:, dh * 512:(dh + 1) * 512]))
                            pairs += [(hts[:, ft, sub * 128:(sub + 1) * 128],
                                       w2q[ri][:, ft, dh * 512:(dh + 1) * 512])
                                      for ft in range(8)]
                            mm_group(S, ps[bank][:], pairs,
                                     ["mo:hts", f"mo:w2q{ri}", "mo:w8T", "mo:b2e"], f"mo_ps{bank}")
                            S.op("scalar", lambda s, sub=sub, dh=dh, bank=bank, yb=yb: s.copy(
                                ysb[yb][:, sub, dh * 512:(dh + 1) * 512], ps[bank][:]),
                                reads=[f"mo_ps{bank}"], writes=[f"mo:y{yb}"])
                    first = (e == 0 and q == 0)
                    S.dma("gpsimd",
                          E["ffacc"][ci * TC:(ci + 1) * TC, :].rearrange("(s p) f -> p s f", p=128),
                          ysb[yb][:],
                          reads=[f"mo:y{yb}"], writes=["ffacc"],
                          **({} if first else {"accum_op": ALU.add}))
                step += 1
    S.barrier()


def ph_moe0s(S, E):
    nc = S.nc
    TC = 512
    NCKE = CAP // TC
    NQ = 4
    FQ = DFF // NQ
    pctx = ExitStack()
    ps = [pctx.enter_context(nc.psum_tensor(f"ms_ps{i}", [128, 512], F32)) for i in range(6)]
    sb = lambda n, shp, dt=BF16: pctx.enter_context(nc.sbuf_tensor("ms_" + n, shp, dt))
    w1q = [sb(f"w1q{i}", [128, 8, FQ]) for i in range(2)]
    w2q = [sb(f"w2q{i}", [128, 8, D]) for i in range(2)]
    xgt = [sb(f"xgt{i}", [128, 8, TC]) for i in range(2)]
    hts = sb("hts", [128, 8, TC])
    b1p_s = sb("b1p", [128, NE * 32], F32)
    b2r_s = sb("b2r", [1, NE * D])
    ones1 = sb("ones1", [1, 128])
    ysb = [sb(f"y{i}", [128, 4, D], F32) for i in range(2)]
    si_sb = sb("si", [128, NT128], mybir.dt.int32)
    wc_sb = sb("wc", [128, NT128], F32)
    gt = [sb(f"gt{i}", [128, D], F32) for i in range(2)]
    with pctx:
        S.dma("sync", b1p_s[:], E["b1p"], writes=["ms:b1p"])
        S.dma("gpsimd", b2r_s[:], E["b2row"], writes=["ms:b2r"])
        S.dma("sync", si_sb[:], E["sidx"], reads=["sidx"], writes=["ms:si"])
        S.dma("sync", wc_sb[:], E["wcol"], reads=["wcol"], writes=["ms:wc"])
        S.op("vector", lambda v: v.memset(ones1[:], 1.0), writes=["ms:ones1"])

        def load_q(e, q, ri):
            S.dma("gpsimd", w1q[ri][:],
                  E["w1"][e].rearrange("(k p) f -> p k f", p=128)[:, :, q * FQ:(q + 1) * FQ],
                  writes=[f"ms:w1q{ri}"])
            S.dma("gpsimd", w2q[ri][:],
                  E["w2"][e][q * FQ:(q + 1) * FQ, :].rearrange("(k p) f -> p k f", p=128),
                  writes=[f"ms:w2q{ri}"])
        load_q(0, 0, 0)
        step = 0
        for e in range(NE):
            for q in range(NQ):
                ri = step % 2
                ne, nq = (e, q + 1) if q + 1 < NQ else (e + 1, 0)
                if ne < NE:
                    load_q(ne, nq, (step + 1) % 2)
                for ci in range(NCKE):
                    xb = ci % 2
                    xk = f"ms:xgt{xb}"
                    base = e * CAP + ci * TC
                    for dt in range(8):
                        S.op("sync", lambda s, dt=dt, base=base, xb=xb: s.dma_start_transpose(
                            xgt[xb][:, dt, :], E["xg"][base:base + TC, dt * 128:(dt + 1) * 128]),
                            reads=["xg"], writes=[xk], dma=True)
                    for ft in range(8):
                        bank = ft % 2
                        pairs = [(w1q[ri][:, dt, ft * 128:(ft + 1) * 128], xgt[xb][:, dt, :])
                                 for dt in range(8)]
                        mm_group(S, ps[bank][:], pairs, [xk, f"ms:w1q{ri}"], f"ms_ps{bank}")
                        gft = q * 8 + ft
                        S.op("scalar", lambda s, ft=ft, bank=bank, gft=gft, e=e: s.activation(
                            hts[:, ft, :], ps[bank][:], AF.Gelu,
                            bias=b1p_s[:, e * 32 + gft:e * 32 + gft + 1]),
                            reads=[f"ms_ps{bank}", "ms:b1p"], writes=["ms:hts"])
                    yb = ci % 2
                    for sub in range(4):
                        for dh in range(2):
                            bank = 2 + dh
                            pairs = []
                            if q == 0:
                                pairs.append((ones1[0:1, :],
                                              b2r_s[0:1, e * D + dh * 512:e * D + (dh + 1) * 512]))
                            pairs += [(hts[:, ft, sub * 128:(sub + 1) * 128],
                                       w2q[ri][:, ft, dh * 512:(dh + 1) * 512])
                                      for ft in range(8)]
                            mm_group(S, ps[bank][:], pairs,
                                     ["ms:hts", f"ms:w2q{ri}", "ms:ones1", "ms:b2r"],
                                     f"ms_ps{bank}")
                            S.op("scalar", lambda s, sub=sub, dh=dh, bank=bank, yb=yb: s.copy(
                                ysb[yb][:, sub, dh * 512:(dh + 1) * 512], ps[bank][:]),
                                reads=[f"ms_ps{bank}"], writes=[f"ms:y{yb}"])
                    S.dma("gpsimd",
                          E["yslots"][base:base + TC, :].rearrange("(s p) f -> p s f", p=128),
                          ysb[yb][:],
                          reads=[f"ms:y{yb}"], writes=["yslots"],
                          **({} if q == 0 else {"accum_op": ALU.add}))
                step += 1
        for i in range(NT128):
            gb = i % 2
            S.op("gpsimd", lambda g, i=i, gb=gb: g.indirect_dma_start(
                gt[gb][:], None, E["yslots"], bass.IndirectOffsetOnAxis(si_sb[:, i:i + 1], 0),
                bounds_check=S.bc_reg(g, 2 * T), oob_is_err=False),
                reads=["yslots", "ms:si"], writes=[f"ms:gt{gb}"], dma=True)
            S.op("vector", lambda v, i=i, gb=gb: v.tensor_scalar(
                gt[gb][:], gt[gb][:], wc_sb[:, i:i + 1], None, ALU.mult),
                reads=[f"ms:gt{gb}", "ms:wc"], writes=[f"ms:gt{gb}"])
            S.dma("sync", E["ffacc"][i * 128:(i + 1) * 128, :], gt[gb][:],
                  reads=[f"ms:gt{gb}"], writes=["ffacc"])
    S.barrier()


def ph_comb0(S, E):
    nc = S.nc
    pctx = ExitStack()
    sb = lambda n, shp, dt=F32: pctx.enter_context(nc.sbuf_tensor("cb_" + n, shp, dt))
    xt = [sb(f"xt{i}", [128, D]) for i in range(2)]
    at = [sb(f"at{i}", [128, D], BF16) for i in range(2)]
    ft = [sb(f"ft{i}", [128, D]) for i in range(2)]
    x1f = sb("x1f", [128, D])
    x1b = sb("x1b", [128, D], BF16)
    ot = sb("ot", [128, D], BF16)
    tmp = {}
    for nm, w in (("sq", D), ("s1", 1), ("s2", 1), ("m", 1), ("ms", 1),
                  ("var", 1), ("rstd", 1), ("nmr", 1)):
        tmp[nm] = sb(nm, [128, w])
        tmp[nm + "_key"] = "cb_" + nm
    with pctx:
        for i in range(NT128):
            bi = i % 2
            sl = slice(i * 128, (i + 1) * 128)
            S.dma("sync", xt[bi][:], E["x_main"][sl, :], writes=[f"cb:x{bi}"])
            S.dma("sync", at[bi][:], E["attn0"][sl, :], reads=["attn0"], writes=[f"cb:a{bi}"])
            S.dma("sync", ft[bi][:], E["ffacc"][sl, :], reads=["ffacc"], writes=[f"cb:f{bi}"])
            S.op("vector", lambda v, bi=bi: v.tensor_add(x1f[:], xt[bi][:], at[bi][:]),
                 reads=[f"cb:x{bi}", f"cb:a{bi}"], writes=["cb:x1f"])
            S.op("vector", lambda v, bi=bi: v.tensor_add(x1f[:], x1f[:], ft[bi][:]),
                 reads=["cb:x1f", f"cb:f{bi}"], writes=["cb:x1f"])
            S.op("vector", lambda v: v.tensor_copy(x1b[:], x1f[:]),
                 reads=["cb:x1f"], writes=["cb:x1b"])
            S.dma("sync", E["x1"][sl, :], x1b[:], reads=["cb:x1b"], writes=["x1"])
            emit_ln_tile(S, E, x1f[:], "cb:x1f", ot[:], "cb:ot", tmp)
            S.dma("sync", E["xn1"][sl, :], ot[:], reads=["cb:ot"], writes=["xn1"])
    S.barrier()


def ph_att1a(S, E):
    nc = S.nc
    pctx = ExitStack()
    ps = [pctx.enter_context(nc.psum_tensor(f"a1_ps{i}", [128, 512], F32)) for i in range(7)]
    pb = pctx.enter_context(nc.psum_tensor("a1_pb", [128, 1024], BF16))
    sb = lambda n, shp, dt=BF16: pctx.enter_context(nc.sbuf_tensor("a1_" + n, shp, dt))
    wqkv_s = sb("wqkv", [128, 8, 3 * D])
    idb = sb("idb", [128, 128])
    xc = [sb(f"xc{i}", [128, 8, CH]) for i in range(2)]
    q3 = sb("q3", [128, H1, DH1], F32)
    rm = sb("rm", [128, H1, 1], F32)
    ss = sb("ss", [128, H1, 1], F32)
    rc = sb("rc", [128, H1, 1], F32)
    qe = sb("qe", [128, H1, DH1], F32)
    qsn = sb("qsn", [128, H1, DH1])
    ek = sb("ek", [128, H1, DH1])
    vsb = sb("v", [128, H1, DH1])
    qT_sb = sb("qT", [128, 8, 128])
    NT_acc = sb("NT", [128, 8, 64], F32)
    D_acc = sb("D", [1, 1024], F32)
    ones128 = sb("o128", [128, 1])
    with pctx:
        S.dma("gpsimd", wqkv_s[:], E["wqkv"].rearrange("(k p) f -> p k f", p=128),
              writes=["a1:wqkv"])
        S.dma("gpsimd", idb[:], E["ident_f32"], writes=["a1:idb"])
        S.op("vector", lambda v: v.memset(NT_acc[:], 0.0), writes=["a1:NT"])
        S.op("vector", lambda v: v.memset(D_acc[:], 0.0), writes=["a1:D"])
        S.op("vector", lambda v: v.memset(ones128[:], 1.0), writes=["a1:o128"])
        for ci in range(NCH):
            bi = ci % 2
            xk = f"a1:xc{bi}"
            S.dma("sync", xc[bi][:], E["xn1T"][:, ci * CH:(ci + 1) * CH]
                  .rearrange("(d p) t -> p d t", p=128), reads=["xn1T"], writes=[xk])
            for part in range(6):
                pairs = [(xc[bi][:, dt, :], wqkv_s[:, dt, part * 512:(part + 1) * 512])
                         for dt in range(8)]
                mm_group(S, ps[part][:], pairs, [xk, "a1:wqkv"], f"a1_ps{part}")
            for half in range(2):
                S.op("scalar", lambda s, half=half: s.copy(
                    q3[:].rearrange("p h d -> p (h d)")[:, half * 512:(half + 1) * 512],
                    ps[half][:]),
                    reads=[f"a1_ps{half}"], writes=["a1:q3"])
            S.op("vector", lambda v: v.tensor_reduce(rm[:], q3[:], AX.X, ALU.max, negate=True),
                 reads=["a1:q3"], writes=["a1:rm"])
            S.op("vector", lambda v: v.tensor_tensor(
                qe[:], *bass.broadcast_tensor_aps(q3[:], rm[:]), ALU.add),
                 reads=["a1:q3", "a1:rm"], writes=["a1:qe"])
            S.op("scalar", lambda s: s.activation(
                qe[:].rearrange("p h d -> p (h d)"), qe[:].rearrange("p h d -> p (h d)"),
                AF.Exp), reads=["a1:qe"], writes=["a1:qe"])
            S.op("vector", lambda v: v.tensor_reduce(ss[:], qe[:], AX.X, ALU.add),
                 reads=["a1:qe"], writes=["a1:ss"])
            S.op("vector", lambda v: v.reciprocal(rc[:].rearrange("p h d -> p (h d)"),
                                                  ss[:].rearrange("p h d -> p (h d)")),
                 reads=["a1:ss"], writes=["a1:rc"])
            S.op("vector", lambda v: v.tensor_tensor(
                qsn[:], *bass.broadcast_tensor_aps(qe[:], rc[:]), ALU.mult),
                 reads=["a1:qe", "a1:rc"], writes=["a1:qsn"])
            for h in range(H1):
                p, r = divmod(h, 2)
                S.op("tensor", lambda t, h=h, p=p, r=r: t.transpose(
                    pb[r * 64:(r + 1) * 64, p * 128:(p + 1) * 128], qsn[:, h, :], idb[:]),
                    reads=["a1:qsn", "a1:idb"], writes=["a1_pb"])
            S.op("scalar", lambda s: s.copy(qT_sb[:].rearrange("p q t -> p (q t)"), pb[:]),
                 reads=["a1_pb"], writes=["a1:qT"])
            S.dma("sync", E["qT1"].rearrange("(q r) t -> r q t", r=128)[:, :, ci * CH:(ci + 1) * CH],
                  qT_sb[:], reads=["a1:qT"], writes=["qT1"])
            for half in range(2):
                S.op("scalar", lambda s, half=half: s.activation(
                    ek[:].rearrange("p h d -> p (h d)")[:, half * 512:(half + 1) * 512],
                    ps[2 + half][:], AF.Exp),
                    reads=[f"a1_ps{2 + half}"], writes=["a1:ek"])
                S.op("scalar", lambda s, half=half: s.copy(
                    vsb[:].rearrange("p h d -> p (h d)")[:, half * 512:(half + 1) * 512],
                    ps[4 + half][:]),
                    reads=[f"a1_ps{4 + half}"], writes=["a1:v"])
            for h in range(H1):
                p, r = divmod(h, 2)
                mm_group(S, ps[6][0:64, 0:64], [(vsb[:, h, :], ek[:, h, :])],
                         ["a1:v", "a1:ek"], "a1_ps6")
                S.op("vector", lambda v, p=p, r=r: v.scalar_tensor_tensor(
                    NT_acc[r * 64:(r + 1) * 64, p, :], NT_acc[r * 64:(r + 1) * 64, p, :],
                    1.0, ps[6][0:64, 0:64], ALU.mult, ALU.add),
                    reads=["a1:NT", "a1_ps6"], writes=["a1:NT"])
            for half in range(2):
                mm_group(S, ps[6][0:1, 0:512],
                         [(ones128[:, 0:1],
                           ek[:].rearrange("p h d -> p (h d)")[:, half * 512:(half + 1) * 512])],
                         ["a1:o128", "a1:ek"], "a1_ps6")
                S.op("vector", lambda v, half=half: v.scalar_tensor_tensor(
                    D_acc[0:1, half * 512:(half + 1) * 512],
                    D_acc[0:1, half * 512:(half + 1) * 512],
                    1.0, ps[6][0:1, 0:512], ALU.mult, ALU.add),
                    reads=["a1:D", "a1_ps6"], writes=["a1:D"])
        S.dma("sync", E["NDa"], NT_acc[:].rearrange("p q d -> p (q d)"),
              reads=["a1:NT"], writes=["NDa"])
        S.dma("sync", E["NDb"], D_acc[:], reads=["a1:D"], writes=["NDb"])
    S.barrier()


def ph_att1b(S, E):
    nc = S.nc
    pctx = ExitStack()
    ps = [pctx.enter_context(nc.psum_tensor(f"b1_ps{i}", [128, 512], F32)) for i in range(6)]
    sb = lambda n, shp, dt=BF16: pctx.enter_context(nc.sbuf_tensor("b1_" + n, shp, dt))
    wout_s = sb("wout", [128, 8, D])
    nda = [sb(f"nda{i}", [128, 512], F32) for i in range(2)]
    ndb = [sb(f"ndb{i}", [1, 1024], F32) for i in range(2)]
    NT_bf = sb("NTbf", [128, 8, 64])
    Dcol = sb("Dcol", [128, 8], F32)
    recD = sb("recD", [128, 8], F32)
    ones11 = sb("ones11", [1, 1], F32)
    M_sb = sb("M", [128, 8, D])
    qc = [sb(f"qc{i}", [128, 8, 128]) for i in range(2)]
    qcs = sb("qcs", [128, 8, 128])
    a1t = [sb(f"a1t{i}", [128, D]) for i in range(2)]
    with pctx:
        S.dma("gpsimd", wout_s[:], E["wout"].rearrange("(q r) f -> r q f", r=128),
              writes=["b1:wout"])
        S.dma("sync", nda[0][:], E["NDa"], reads=["NDa"], writes=["b1:nda0"])
        S.dma("sync", nda[1][:], E["NDa_sib"], reads=["NDa_sib"], writes=["b1:nda1"])
        S.dma("sync", ndb[0][:], E["NDb"], reads=["NDb"], writes=["b1:ndb0"])
        S.dma("sync", ndb[1][:], E["NDb_sib"], reads=["NDb_sib"], writes=["b1:ndb1"])
        S.op("vector", lambda v: v.memset(ones11[:], 1.0), writes=["b1:ones11"])
        S.op("vector", lambda v: v.tensor_add(nda[0][:], nda[0][:], nda[1][:]),
             reads=["b1:nda0", "b1:nda1"], writes=["b1:nda0"])
        S.op("vector", lambda v: v.tensor_copy(NT_bf[:].rearrange("p q d -> p (q d)"),
                                               nda[0][:]),
             reads=["b1:nda0"], writes=["b1:NTbf"])
        S.op("vector", lambda v: v.tensor_add(ndb[0][:], ndb[0][:], ndb[1][:]),
             reads=["b1:ndb0", "b1:ndb1"], writes=["b1:ndb0"])
        for h in range(H1):
            p, r = divmod(h, 2)
            mm_group(S, ps[5][0:64, 0:1], [(ndb[0][0:1, h * 64:(h + 1) * 64], ones11[0:1, 0:1])],
                     ["b1:ndb0", "b1:ones11"], "b1_ps5")
            S.op("scalar", lambda s, p=p, r=r: s.copy(Dcol[r * 64:(r + 1) * 64, p:p + 1],
                                                      ps[5][0:64, 0:1]),
                 reads=["b1_ps5"], writes=["b1:Dcol"])
        S.op("vector", lambda v: v.reciprocal(recD[:], Dcol[:]),
             reads=["b1:Dcol"], writes=["b1:recD"])
        for h in range(H1):
            p, r = divmod(h, 2)
            for dh in range(2):
                bank = dh
                mm_group(S, ps[bank][0:64, :],
                         [(NT_bf[r * 64:(r + 1) * 64, p, :],
                           wout_s[r * 64:(r + 1) * 64, p, dh * 512:(dh + 1) * 512])],
                         ["b1:NTbf", "b1:wout"], f"b1_ps{bank}")
                S.op("scalar", lambda s, p=p, r=r, dh=dh, bank=bank: s.copy(
                    M_sb[r * 64:(r + 1) * 64, p, dh * 512:(dh + 1) * 512], ps[bank][0:64, :]),
                    reads=[f"b1_ps{bank}"], writes=["b1:M"])
        for ci in range(NCH):
            bi = ci % 2
            S.dma("sync", qc[bi][:],
                  E["qT1"].rearrange("(q r) t -> r q t", r=128)[:, :, ci * CH:(ci + 1) * CH],
                  reads=["qT1"], writes=[f"b1:qc{bi}"])
            for p in range(8):
                S.op("vector", lambda v, p=p, bi=bi: v.tensor_scalar(
                    qcs[:, p, :], qc[bi][:, p, :], recD[:, p:p + 1], DH1 ** -0.5,
                    ALU.mult, ALU.mult),
                    reads=[f"b1:qc{bi}", "b1:recD"], writes=["b1:qcs"])
            for dh in range(2):
                bank = 2 + dh
                pairs = [(qcs[:, p, :], M_sb[:, p, dh * 512:(dh + 1) * 512])
                         for p in range(8)]
                mm_group(S, ps[bank][:], pairs, ["b1:qcs", "b1:M"], f"b1_ps{bank}")
                S.op("scalar", lambda s, dh=dh, bank=bank, bi=bi: s.copy(
                    a1t[bi][:, dh * 512:(dh + 1) * 512], ps[bank][:]),
                    reads=[f"b1_ps{bank}"], writes=[f"b1:a1t{bi}"])
            S.dma("sync", E["attn1"][ci * CH:(ci + 1) * CH, :], a1t[bi][:],
                  reads=[f"b1:a1t{bi}"], writes=["attn1"])
    S.barrier()


def ph_ffn1(S, E):
    nc = S.nc
    TC = 512
    NCK = T // TC
    pctx = ExitStack()
    ps = [pctx.enter_context(nc.psum_tensor(f"f1_ps{i}", [128, 512], F32)) for i in range(6)]
    sb = lambda n, shp, dt=BF16: pctx.enter_context(nc.sbuf_tensor("f1_" + n, shp, dt))
    w1t = sb("w1t", [128, 8, DFF])
    w2t = sb("w2t", [128, 32, D])
    ones1b = sb("ones1b", [1, 128])
    b1f_s = sb("b1f", [128, 32], F32)
    b2f_s = sb("b2f", [1, D])
    thr = sb("thr", [128, 2], F32)
    nthr = sb("nthr", [128, 2], F32)
    with pctx:
        S.dma("sync", b1f_s[:], E["b1fp"], writes=["f1:b1f"])
        S.dma("gpsimd", b2f_s[:], E["b2f"], writes=["f1:b2f"])
        S.op("vector", lambda v: v.memset(ones1b[:], 1.0), writes=["f1:ones1b"])
        actx = ExitStack()
        sba = lambda n, shp, dt=BF16: actx.enter_context(nc.sbuf_tensor("f1a_" + n, shp, dt))
        wld = [sba(f"wld{i}", [128, DFF], F32) for i in range(2)]
        acc1 = sba("acc1", [128, 1], F32)
        acc2 = sba("acc2", [128, 1], F32)
        red = sba("red", [128, 1], F32)
        onescol = sba("onescol", [128, 1], F32)
        scal = sba("scal", [1, 2], F32)
        pos = sba("pos", [128, DFF])
        neg = sba("neg", [128, DFF])
        with actx:
            S.op("vector", lambda v: v.memset(acc1[:], 0.0), writes=["f1:acc1"])
            S.op("vector", lambda v: v.memset(acc2[:], 0.0), writes=["f1:acc2"])
            S.op("vector", lambda v: v.memset(onescol[:], 1.0), writes=["f1:onescol"])
            for mat, acc, acck, nt in (("w1T", acc1, "f1:acc1", 8), ("w2T", acc2, "f1:acc2", 32)):
                for i in range(nt):
                    bi = i % 2
                    w = DFF if mat == "w1T" else D
                    S.dma("sync", wld[bi][:, 0:w], E[mat][i * 128:(i + 1) * 128, :],
                          writes=[f"f1:wld{bi}"])
                    S.op("vector", lambda v, bi=bi, w=w: v.tensor_reduce(
                        red[:], wld[bi][:, 0:w], AX.X, ALU.add, apply_absolute_value=True),
                        reads=[f"f1:wld{bi}"], writes=["f1:red"])
                    S.op("vector", lambda v, acc=acc: v.tensor_add(acc[:], acc[:], red[:]),
                         reads=["f1:red", acck], writes=[acck])
            NW = DFF * D
            for j, (acc, acck) in enumerate(((acc1, "f1:acc1"), (acc2, "f1:acc2"))):
                mm_group(S, ps[0][0:1, 0:1], [(acc[:, 0:1], onescol[:, 0:1])],
                         [acck, "f1:onescol"], "f1_ps0")
                S.op("vector", lambda v, j=j: v.tensor_scalar(
                    scal[0:1, j:j + 1], ps[0][0:1, 0:1], 0.5 / NW, 0.5e-8, ALU.mult, ALU.max),
                    reads=["f1_ps0"], writes=["f1:scal"])
            onesrow = sba("onesrow", [1, 128], F32)
            S.op("vector", lambda v: v.memset(onesrow[:], 1.0), writes=["f1:onesrow"])
            mm_group(S, ps[0][:, 0:2], [(onesrow[0:1, :], scal[0:1, 0:2])],
                     ["f1:onesrow", "f1:scal"], "f1_ps0")
            S.op("scalar", lambda s: s.copy(thr[:], ps[0][:, 0:2]), reads=["f1_ps0"],
                 writes=["f1:thr"])
            S.op("vector", lambda v: v.tensor_scalar(nthr[:], thr[:], -1.0, None, ALU.mult),
                 reads=["f1:thr"], writes=["f1:nthr"])
            for mat, dstk, nt in (("w1T", "f1:w1t", 8), ("w2T", "f1:w2t", 32)):
                j = 0 if mat == "w1T" else 1
                for i in range(nt):
                    bi = i % 2
                    w = DFF if mat == "w1T" else D
                    S.dma("sync", wld[bi][:, 0:w], E[mat][i * 128:(i + 1) * 128, :],
                          writes=[f"f1:wld{bi}"])
                    S.op("vector", lambda v, bi=bi, w=w, j=j: v.tensor_scalar(
                        pos[:, 0:w], wld[bi][:, 0:w], thr[:, j:j + 1], None, ALU.is_ge),
                        reads=[f"f1:wld{bi}", "f1:thr"], writes=["f1:pos"])
                    S.op("vector", lambda v, bi=bi, w=w, j=j: v.tensor_scalar(
                        neg[:, 0:w], wld[bi][:, 0:w], nthr[:, j:j + 1], None, ALU.is_le),
                        reads=[f"f1:wld{bi}", "f1:nthr"], writes=["f1:neg"])
                    dst = w1t[:, i, :] if mat == "w1T" else w2t[:, i, :]
                    S.op("vector", lambda v, dst=dst, w=w: v.tensor_sub(dst, pos[:, 0:w],
                                                                        neg[:, 0:w]),
                         reads=["f1:pos", "f1:neg"], writes=[dstk])
        S.barrier()
        bctx = ExitStack()
        sbb = lambda n, shp, dt=BF16: bctx.enter_context(nc.sbuf_tensor("f1b_" + n, shp, dt))
        xcs = [sbb(f"xc{i}", [128, 8, TC]) for i in range(2)]
        hts = sbb("hts", [128, 32, TC])
        a1l = [sbb(f"a1l{i}", [128, D]) for i in range(2)]
        x1l = [sbb(f"x1l{i}", [128, D]) for i in range(2)]
        yt = [sbb(f"yt{i}", [128, D], F32) for i in range(2)]
        with bctx:
            for ci in range(NCK):
                xb = ci % 2
                xk = f"f1:xc{xb}"
                S.dma("sync", xcs[xb][:], E["xn1T"][:, ci * TC:(ci + 1) * TC]
                      .rearrange("(d p) t -> p d t", p=128), reads=["xn1T"], writes=[xk])
                for ft in range(32):
                    bank = ft % 2
                    pairs = [(w1t[:, dt, ft * 128:(ft + 1) * 128], xcs[xb][:, dt, :])
                             for dt in range(8)]
                    mm_group(S, ps[bank][:], pairs, [xk, "f1:w1t"], f"f1_ps{bank}")
                    S.op("scalar", lambda s, ft=ft, bank=bank: s.activation(
                        hts[:, ft, :], ps[bank][:], AF.Gelu, bias=b1f_s[:, ft:ft + 1]),
                        reads=[f"f1_ps{bank}", "f1:b1f"], writes=["f1:hts"])
                for sub in range(4):
                    ti = sub % 2
                    sl = slice(ci * TC + sub * 128, ci * TC + (sub + 1) * 128)
                    S.dma("sync", a1l[ti][:], E["attn1"][sl, :], reads=["attn1"],
                          writes=[f"f1:a1l{ti}"])
                    S.dma("sync", x1l[ti][:], E["x1"][sl, :], reads=["x1"],
                          writes=[f"f1:x1l{ti}"])
                    for dh in range(2):
                        bank = 2 + dh
                        pairs = [(ones1b[0:1, :], b2f_s[0:1, dh * 512:(dh + 1) * 512])]
                        pairs += [(hts[:, ft, sub * 128:(sub + 1) * 128],
                                   w2t[:, ft, dh * 512:(dh + 1) * 512]) for ft in range(32)]
                        mm_group(S, ps[bank][:], pairs,
                                 ["f1:hts", "f1:w2t", "f1:ones1b", "f1:b2f"], f"f1_ps{bank}")
                        S.op("vector", lambda v, dh=dh, bank=bank, ti=ti: v.tensor_tensor(
                            yt[ti][:, dh * 512:(dh + 1) * 512], ps[bank][:],
                            a1l[ti][:, dh * 512:(dh + 1) * 512], ALU.add),
                            reads=[f"f1_ps{bank}", f"f1:a1l{ti}"], writes=[f"f1:yt{ti}"])
                    S.op("vector", lambda v, ti=ti: v.tensor_add(yt[ti][:], yt[ti][:],
                                                                 x1l[ti][:]),
                         reads=[f"f1:yt{ti}", f"f1:x1l{ti}"], writes=[f"f1:yt{ti}"])
                    S.dma("sync", E["y"][sl, :], yt[ti][:], reads=[f"f1:yt{ti}"], writes=["y"])
    S.barrier()



def host_input_shapes():
    return {
        "x_main": ((T, D), F32), "x_pre": ((T, D), F32),
        "smask": ((128, 1), F32), "ident_f32": ((128, 128), F32),
        "cosm": ((T, 64), BF16), "sinm": ((T, 64), BF16),
        "cosTm": ((64, T), BF16), "sinTm": ((64, T), BF16),
        "cosp": ((T, 64), BF16), "sinp": ((T, 64), BF16),
        "dmask": ((NH, 128, 128), BF16), "gpow": ((NH, 128, 128), BF16),
        "wdec": ((128, NH), F32), "gnwb": ((128, DV), F32),
        "wq": ((D, D), F32), "wk": ((D, D), F32), "wv": ((D, VDIM), F32),
        "wg": ((D, VDIM), F32), "wo": ((VDIM, D), F32),
        "gate_w": ((D, NE), F32),
        "w1": ((NE, D, DFF), F32), "b1p": ((128, NE * 32), F32),
        "w2": ((NE, DFF, D), F32), "b2e": ((NE, D), F32),
        "b2row": ((1, NE * D), F32),
        "Ltri": ((128, 128), F32), "eoff": ((1, NE), F32),
        "wqkv": ((D, 3 * D), F32), "wout": ((D, D), F32),
        "w1T": ((D, DFF), F32), "b1fp": ((128, 32), F32),
        "w2T": ((DFF, D), F32), "b2f": ((1, D), F32),
    }


def _perm_evens_odds():
    p = []
    for h in range(NH):
        base = h * DQK
        p += list(range(base, base + DQK, 2))
        p += list(range(base + 1, base + DQK, 2))
    return np.array(p)


def host_prep(inputs):
    x = np.ascontiguousarray(inputs["L0_attn_hidden_states"].reshape(B, L, D))
    perm = _perm_evens_odds()
    c64 = np.float64
    inv = 1.0 / 10000.0 ** (np.arange(0, DQK, 2, dtype=c64) / DQK)
    ang = np.arange(L, dtype=c64)[:, None] * inv[None, :]
    cos_all = np.cos(ang).astype(np.float32)
    sin_all = np.sin(ang).astype(np.float32)
    s = np.log2(1.0 - np.exp2(-5.0 - np.arange(NH, dtype=c64)))
    ii = np.arange(CH, dtype=c64)
    dmask = np.zeros((NH, CH, CH), np.float32)
    gpow = np.zeros((NH, CH, CH), np.float32)
    wdec = np.zeros((CH, NH), np.float32)
    for h in range(NH):
        M = np.where(ii[None, :] >= ii[:, None], np.exp2((ii[None, :] - ii[:, None]) * s[h]), 0.0)
        dmask[h] = (M * DQK ** -0.5).astype(np.float32)
        gp = np.exp2(ii * s[h]) * DQK ** -0.5
        gpow[h] = np.broadcast_to(gp[None, :], (CH, CH)).astype(np.float32)
        wdec[:, h] = np.exp2((CH - ii) * s[h]).astype(np.float32)
    bf = lambda a: np.ascontiguousarray(a.astype(ml_dtypes.bfloat16))
    common = {
        "ident_f32": np.eye(128, dtype=np.float32),
        "cosp": bf(cos_all[0:T]),
        "sinp": bf(sin_all[0:T]),
        "dmask": bf(dmask), "gpow": bf(gpow), "wdec": wdec,
        "gnwb": np.ascontiguousarray(np.broadcast_to(
            inputs["L0_attn_g_norm_w"][None, :], (128, DV))).astype(np.float32),
        "wq": np.ascontiguousarray(inputs["L0_attn_Wq"][:, perm]),
        "wk": np.ascontiguousarray(inputs["L0_attn_Wk"][:, perm]),
        "wv": np.ascontiguousarray(inputs["L0_attn_Wv"]),
        "wg": np.ascontiguousarray(inputs["L0_attn_Wg"]),
        "wo": np.ascontiguousarray(inputs["L0_attn_Wo"]),
        "gate_w": np.ascontiguousarray(inputs["L0_ffn_gate_w"]),
        "w1": np.ascontiguousarray(inputs["L0_ffn_w1"]),
        "b1p": np.ascontiguousarray(
            inputs["L0_ffn_b1"].reshape(NE, 32, 128).transpose(2, 0, 1).reshape(128, NE * 32)),
        "w2": np.ascontiguousarray(inputs["L0_ffn_w2"]),
        "b2e": np.ascontiguousarray(inputs["L0_ffn_b2"]),
        "b2row": np.ascontiguousarray(inputs["L0_ffn_b2"].reshape(1, NE * D)),
        "Ltri": (np.arange(128)[:, None] < np.arange(128)[None, :]).astype(np.float32),
        "eoff": (np.arange(NE, dtype=np.float32) * CAP).reshape(1, NE),
        "wqkv": np.ascontiguousarray(inputs["L1_attn_W_qkv"]),
        "wout": np.ascontiguousarray(inputs["L1_attn_W_out"]),
        "w1T": np.ascontiguousarray(inputs["L1_ffn_W1"].T),
        "b1fp": np.ascontiguousarray(inputs["L1_ffn_b1"].reshape(32, 128).T),
        "w2T": np.ascontiguousarray(inputs["L1_ffn_W2"].T),
        "b2f": np.ascontiguousarray(inputs["L1_ffn_b2"][None, :]),
    }
    per_core = []
    for c in range(NCORE):
        b, hf = c // 2, c % 2
        d = dict(common)
        d["x_main"] = np.ascontiguousarray(x[b, hf * T:(hf + 1) * T])
        d["x_pre"] = np.ascontiguousarray(x[b, 0:T])
        d["smask"] = np.full((128, 1), float(hf), np.float32)
        d["cosm"] = bf(cos_all[hf * T:(hf + 1) * T])
        d["sinm"] = bf(sin_all[hf * T:(hf + 1) * T])
        d["cosTm"] = bf(cos_all[hf * T:(hf + 1) * T].T)
        d["sinTm"] = bf(sin_all[hf * T:(hf + 1) * T].T)
        per_core.append(d)
    return per_core


def kernel(**inputs):
    per_core = host_prep(inputs)
    results = [dict() for _ in range(NCORE)]
    for gi, group in enumerate(GROUPS):
        nc, ext_in, ext_out = build(group)
        in_maps = []
        for c in range(NCORE):
            m = {}
            for nm in ext_in:
                if nm.endswith("_sib"):
                    m[nm] = results[c ^ 1][nm[:-4]]
                elif nm in per_core[c]:
                    m[nm] = per_core[c][nm]
                else:
                    m[nm] = results[c][nm]
            in_maps.append(m)
        res = run_bass_kernel_spmd(nc, in_maps, core_ids=list(range(NCORE)))
        for c in range(NCORE):
            for nm in ext_out:
                results[c][nm] = np.asarray(res.results[c][nm])
    y = np.zeros((B, L, D), np.float32)
    for c in range(NCORE):
        b, hf = c // 2, c % 2
        y[b, hf * T:(hf + 1) * T] = results[c]["y"]
    return y
